# version 04
# baseline (speedup 1.0000x reference)
import math
import numpy as np
import ml_dtypes
import concourse.bass as bass
import concourse.mybir as mybir
from concourse.bass_utils import run_bass_kernel_spmd

F32 = mybir.dt.float32
BF16 = mybir.dt.bfloat16
AF = mybir.ActivationFunctionType
ALU = mybir.AluOpType
AX = mybir.AxisListType

D = 4096
SEQ = 2048
T = 512
NT = SEQ // T
DC = D // 128
NQ, NKV, HD = 32, 8, 128
GQ = NQ // NKV
INTER = 11008
NIC = INTER // 128
G = 4
GROUPS = [list(range(s, min(s + G, NIC))) for s in range(0, NIC, G)]
SCALE = 1.0 / math.sqrt(HD)
NEG = -1.0e30
ENGINES = ("pe", "act", "dve", "sync", "pool")


class Prog:

    def __init__(self):
        self.ops = {e: [] for e in ENGINES}
        self.cnt = {e: 0 for e in ENGINES}
        self.waited = {e: {s: 0 for s in ENGINES} for e in ENGINES}
        self.region = {}

    def wait(self, eng, sem, val):
        if val is None or val <= 0:
            return
        if self.waited[eng][sem] >= val:
            return
        self.waited[eng][sem] = val
        self.ops[eng].append(("w", sem, val))

    def op(self, eng, fn, inc=False, n=1):
        if inc:
            self.cnt[eng] += n
        self.ops[eng].append(("o", fn, inc, n))
        return self.cnt[eng]

    def use_region(self, eng, *names):
        for nm in names:
            ev = self.region.get(nm)
            if ev is not None:
                self.wait(eng, ev[0], ev[1])

    def free_region(self, nm, sem, cnt):
        self.region[nm] = (sem, cnt)

    def verify(self):
        pos = {e: 0 for e in ENGINES}
        sim = {e: 0 for e in ENGINES}
        progress = True
        while progress:
            progress = False
            for e in ENGINES:
                while pos[e] < len(self.ops[e]):
                    rec = self.ops[e][pos[e]]
                    if rec[0] == "w" and sim[rec[1]] < rec[2]:
                        break
                    if rec[0] == "o" and rec[2]:
                        sim[e] += rec[3]
                    pos[e] += 1
                    progress = True
        stuck = {}
        for e in ENGINES:
            if pos[e] < len(self.ops[e]):
                rec = self.ops[e][pos[e]]
                stuck[e] = f"at {pos[e]}/{len(self.ops[e])} waiting {rec[1]}>={rec[2]} (have {sim[rec[1]]})"
        if stuck:
            raise RuntimeError(f"schedule deadlock: {stuck}")

    def materialize(self, nc, sems):
        with nc.Block() as block:
            def runner(eng_name):
                def run(e):
                    for rec in self.ops[eng_name]:
                        if rec[0] == "w":
                            e.wait_ge(sems[rec[1]], rec[2])
                        else:
                            ins = rec[1](e)
                            if rec[2]:
                                ins.then_inc(sems[eng_name], rec[3])
                return run

            block.tensor(runner("pe"))
            block.scalar(runner("act"))
            block.vector(runner("dve"))
            block.sync(runner("sync"))
            block.gpsimd(runner("pool"))


def build_program(nt=NT, phases=7, taps=()):
    nc = bass.Bass("TRN2", target_bir_lowering=False, dynamic_dma_scratch_size=16384)

    xh = nc.dram_tensor("x", [D, SEQ], F32, kind="ExternalInput").ap()
    wqh = nc.dram_tensor("wq", [NQ, 128, D], F32, kind="ExternalInput").ap()
    wkh = nc.dram_tensor("wk", [NKV, 128, D], F32, kind="ExternalInput").ap()
    wvh = nc.dram_tensor("wv", [DC, 128, NKV * HD], F32, kind="ExternalInput").ap()
    woh = nc.dram_tensor("wo", [DC, 128, D], F32, kind="ExternalInput").ap()
    wgh = nc.dram_tensor("wg", [NIC, 128, D], F32, kind="ExternalInput").ap()
    wuh = nc.dram_tensor("wu", [NIC, 128, D], F32, kind="ExternalInput").ap()
    wdh = nc.dram_tensor("wd", [NIC, 128, D], F32, kind="ExternalInput").ap()
    agh = nc.dram_tensor("ag", [128, NIC], F32, kind="ExternalInput").ap()
    auh = nc.dram_tensor("au", [128, NIC], F32, kind="ExternalInput").ap()
    adh = nc.dram_tensor("ad", [128, DC], F32, kind="ExternalInput").ap()
    maskh = nc.dram_tensor("mask", [128, 256], F32, kind="ExternalInput").ap()
    idh = nc.dram_tensor("idm", [128, 128], BF16, kind="ExternalInput").ap()
    onbh = nc.dram_tensor("onb", [128, 1], BF16, kind="ExternalInput").ap()
    onfh = nc.dram_tensor("onf", [1, 128], F32, kind="ExternalInput").ap()
    outh = nc.dram_tensor("out", [D, SEQ], F32, kind="ExternalOutput").ap()
    wq_c = nc.dram_tensor("wq_c", [NQ, 128, D], BF16, kind="Internal").ap()
    wk_c = nc.dram_tensor("wk_c", [NKV, 128, D], BF16, kind="Internal").ap()
    wv_c = nc.dram_tensor("wv_c", [DC, 128, NKV * HD], BF16, kind="Internal").ap()
    wo_c = nc.dram_tensor("wo_c", [DC, 128, D], BF16, kind="Internal").ap()
    wg_c = nc.dram_tensor("wg_c", [NIC, 128, D], BF16, kind="Internal").ap()
    wu_c = nc.dram_tensor("wu_c", [NIC, 128, D], BF16, kind="Internal").ap()
    wd_c = nc.dram_tensor("wd_c", [NIC, 128, D], BF16, kind="Internal").ap()
    tap_out = {}
    for tname, tshape in taps:
        tap_out[tname] = nc.dram_tensor("tap_" + tname, tshape, F32, kind="ExternalOutput").ap()

    import contextlib
    ctx = contextlib.ExitStack()
    sb = lambda name, shape, dt: ctx.enter_context(nc.sbuf_tensor(name, shape, dt))
    ps = lambda name, shape, dt: ctx.enter_context(nc.psum_tensor(name, shape, dt))

    x1 = sb("x1", [128, DC * T], F32)
    xn = sb("xn", [128, DC * T], BF16)
    big = sb("big", [128, max(NQ * T, G * 4096)], BF16)
    kcache = sb("kcache", [128, NKV * 1024], BF16)
    vcache = sb("vcache", [128, 8 * NKV * HD], BF16)
    ring = sb("ring", [128, 3 * 4096], BF16)
    hgrp = sb("hgrp", [128, G * T], BF16)
    Pb = sb("Pb", [128, 2 * 640], BF16)
    PTb = sb("PTb", [128, 2 * 640], BF16)
    Pn = sb("Pn", [128, 2 * 640], BF16)
    hsb = sb("hsb", [128, T], BF16)
    sqb = sb("sqb", [128, T], BF16)
    tmpbuf = sb("tmpbuf", [128, 6], F32)
    strow = sb("strow", [1, 2 * T], F32)
    maskb = sb("maskb", [128, 256], F32)
    rstdb = sb("rstdb", [128, T], F32)
    mrb = sb("mrb", [128, T], BF16)
    idb = sb("idb", [128, 128], BF16)
    onb = sb("onb_s", [128, 1], BF16)
    onf = sb("onf_s", [1, 128], F32)
    scales = sb("scales", [128, 2 * NIC + DC], F32)
    agb = scales[:, 0:NIC]
    aub = scales[:, NIC:2 * NIC]
    adb = scales[:, 2 * NIC:2 * NIC + DC]
    rs = tmpbuf[:, 0:3]
    rrs = tmpbuf[:, 3:6]
    tmp = Pb[:, 0:1024].bitcast(F32)[:, 0:T]
    epsb = sb("epsb", [1, 1], F32)

    pAB = ps("pAB", [128, 1024], F32)
    pCD = ps("pCD", [128, 1024], F32)
    pEF = ps("pEF", [128, 1024], F32)
    pO = ps("pO", [128, 1024], F32)

    sems = {e: ctx.enter_context(nc.semaphore(f"sem_{e}")) for e in ENGINES}

    p = Prog()

    def dma(eng, out, in_):
        return p.op(eng, lambda e, o=out, i=in_: e.dma_start(o, i), inc=True, n=16)

    def mm(out, lhsT, rhs, start, stop, inc=False):
        return p.op("pe", lambda e, o=out, l=lhsT, r=rhs, st=start, sp=stop:
                    e.matmul(o, l, r, start=st, stop=sp), inc=inc)

    def slc(t, c0, n):
        return t[:, c0:c0 + n]

    halfslots = [(pAB, 0), (pAB, 512), (pCD, 0), (pCD, 512), (pEF, 0), (pEF, 512), (pO, 0), (pO, 512)]
    halfname = ["pAB0", "pAB1", "pCD0", "pCD1", "pEF0", "pEF1", "pO0", "pO1"]

    for dst, src in ((maskb[:, :], maskh), (idb[:, :], idh), (onb[:, :], onbh), (onf[:, :], onfh),
                     (agb, agh), (aub, auh), (adb, adh)):
        dma("sync", dst, src)
    cst = p.cnt["sync"]
    for e in ("pe", "dve", "act", "pool"):
        p.wait(e, "sync", cst)
    eps_done = p.op("dve", lambda e: e.memset(epsb[:, :], 1.0e-5), inc=True)
    p.wait("act", "dve", eps_done)

    ring_free = [None, None, None]
    ring_next = [0]

    def ring_load(src_ap, ncols=4096, q="pool", cache_dst=None):
        r = ring_next[0]
        ring_next[0] = (r + 1) % 3
        if ring_free[r] is not None:
            p.wait(q, ring_free[r][0], ring_free[r][1])
        done = dma(q, slc(ring, r * 4096, ncols), src_ap)
        if cache_dst is not None:
            p.wait(q, q, done)
            wb = dma(q, cache_dst, slc(ring, r * 4096, ncols))
            ring_free[r] = (q, wb)
            return r, done, (q, wb)
        return r, done, None

    def ln(x1_ready):
        p.use_region("act", "xn")
        st_done = 0
        for dc in range(DC):
            if x1_ready is not None:
                p.wait("act", x1_ready[dc][0], x1_ready[dc][1])
            cast_done = p.op("act", lambda e, d=dc: e.copy(slc(xn, d * T, T), slc(x1, d * T, T)), inc=True)
            p.use_region("dve", "sqb")
            p.wait("dve", "act", cast_done)
            sq_done = p.op("dve", lambda e, d=dc: e.tensor_mul(sqb[:, :], slc(xn, d * T, T), slc(xn, d * T, T)), inc=True)
            p.wait("pe", "act", cast_done)
            if dc == 0:
                p.use_region("pe", "pAB0", "pAB1")
            mm(pAB[0:1, 0:T], onb[:, 0:1], slc(xn, dc * T, T), dc == 0, dc == DC - 1)
            p.wait("pe", "dve", sq_done)
            st_done = mm(pAB[0:1, 512:512 + T], onb[:, 0:1], sqb[:, :], dc == 0, dc == DC - 1, inc=True)
            p.free_region("sqb", "pe", st_done)
        p.wait("dve", "pe", st_done)
        p.use_region("dve", "strow")
        r0, r1 = strow[0:1, 0:T], strow[0:1, T:2 * T]
        rt = tmp[0:1, 0:T]
        p.op("dve", lambda e: e.tensor_scalar_mul(r0, pAB[0:1, 0:T], 1.0 / D))
        fr0 = p.op("dve", lambda e: e.tensor_scalar_mul(r1, pAB[0:1, 512:512 + T], 1.0 / D), inc=True)
        p.free_region("pAB0", "dve", fr0)
        p.free_region("pAB1", "dve", fr0)
        p.op("dve", lambda e: e.tensor_mul(rt, r0, r0))
        var_done = p.op("dve", lambda e: e.tensor_sub(r1, r1, rt), inc=True)
        p.wait("act", "dve", var_done)
        p.op("act", lambda e: e.activation(r1, r1, AF.Ln, bias=epsb[0:1, 0:1]))
        rstd_done = p.op("act", lambda e: e.activation(r1, r1, AF.Exp, scale=-0.5), inc=True)
        p.wait("dve", "act", rstd_done)
        mr_done = p.op("dve", lambda e: e.tensor_mul(r0, r0, r1), inc=True)
        p.wait("pe", "dve", mr_done)
        p.use_region("pe", "pCD0", "pCD1")
        mm(pCD[:, 0:T], onf[0:1, :], r1, True, True)
        bc_done = mm(pCD[:, 512:512 + T], onf[0:1, :], r0, True, True, inc=True)
        p.free_region("strow", "pe", bc_done)
        p.wait("dve", "pe", bc_done)
        p.op("dve", lambda e: e.tensor_copy(rstdb[:, :], pCD[:, 0:T]))
        ev_done = p.op("dve", lambda e: e.tensor_copy(mrb[:, :], pCD[:, 512:512 + T]), inc=True)
        p.free_region("pCD0", "dve", ev_done)
        p.free_region("pCD1", "dve", ev_done)
        norm_done = {}
        for dc in range(DC):
            p.op("dve", lambda e, d=dc: e.tensor_mul(tmp, slc(x1, d * T, T), rstdb[:, :]))
            norm_done[dc] = p.op("dve", lambda e, d=dc:
                                 e.tensor_sub(slc(xn, d * T, T), tmp, mrb[:, :]), inc=True)
        return norm_done

    def emit_tile(i, phases):
        t0 = i * T
        par = i % 2

        p.use_region("sync", "x1")
        xload = dma("sync", x1[:, :].rearrange("p (dc t) -> p dc t", t=T),
                    xh.rearrange("(dc p) t -> p dc t", p=128)[:, :, t0:t0 + T])

        norm_done = ln({dc: ("sync", xload) for dc in range(DC)})

        def tap(name, ap_src, sem, cnt):
            if name not in tap_out:
                return 0
            p.wait("pool", sem, cnt)
            return dma("pool", tap_out[name], ap_src)

        tap("xn1", xn[:, :], "dve", norm_done[DC - 1])

        wq_src = (lambda c: (wqh[c], "pool", wq_c[c])) if i == 0 else (lambda c: (wq_c[c], "sync", None))
        wk_src = (lambda c: (wkh[c], "pool", wk_c[c])) if i == 0 else (lambda c: (wk_c[c], "sync", None))
        wv_src = (lambda c: (wvh[c], "pool", wv_c[c])) if i == 0 else (lambda c: (wv_c[c], "sync", None))
        wo_src = (lambda c: (woh[c], "pool", wo_c[c])) if i == 0 else (lambda c: (wo_c[c], "sync", None))
        wg_src = (lambda c: (wgh[c], "pool", wg_c[c])) if i <= 1 else (lambda c: (wg_c[c], "sync", None))
        wu_src = (lambda c: (wuh[c], "pool", wu_c[c])) if i <= 1 else (lambda c: (wu_c[c], "sync", None))
        if i == 1:
            pass
        for dc in range(DC):
            sa, qq_, cd = wv_src(dc)
            r, wv_done, _ = ring_load(sa, ncols=NKV * HD, q=qq_, cache_dst=cd)
            p.wait("pe", qq_, wv_done)
            p.wait("pe", "dve", norm_done[dc])
            if dc == 0:
                p.use_region("pe", *halfname)
            for tc in range(4):
                for half in range(2):
                    bank, off = halfslots[tc * 2 + half]
                    last = mm(bank[:, off:off + T],
                              xn[:, dc * T + tc * 128: dc * T + tc * 128 + 128],
                              slc(ring, r * 4096 + half * T, T),
                              dc == 0, dc == DC - 1,
                              inc=(dc == DC - 1 or (tc == 3 and half == 1)))
            ring_free[r] = ("pe", last)
        vmm_done = p.cnt["pe"]
        for s8 in range(8):
            tc, half = s8 // 2, s8 % 2
            bank, off = halfslots[s8]
            vslot = par * 4 + tc
            p.wait("act", "pe", vmm_done)
            p.use_region("act", "vcache")
            ev = p.op("act", lambda e, b=bank, o=off, vs=vslot, hf=half:
                      e.copy(vcache[:, vs * 1024 + hf * T: vs * 1024 + hf * T + T], b[:, o:o + T]), inc=True)
            p.free_region(halfname[s8], "act", ev)

        for hk in range(NKV):
            sa, qq_, cd = wk_src(hk)
            r, wk_done, _ = ring_load(sa, q=qq_, cache_dst=cd)
            p.wait("pe", qq_, wk_done)
            nm = halfname[hk % 2]
            p.use_region("pe", nm)
            bank, off = halfslots[hk % 2]
            for dc in range(DC):
                last = mm(bank[:, off:off + T], slc(ring, r * 4096 + dc * 128, 128),
                          slc(xn, dc * T, T), dc == 0, dc == DC - 1, inc=(dc == DC - 1))
            ring_free[r] = ("pe", last)
            p.wait("act", "pe", last)
            p.use_region("act", "kcache")
            ev = p.op("act", lambda e, b=bank, o=off, h=hk:
                      e.copy(kcache[:, h * 1024 + par * T: h * 1024 + par * T + T], b[:, o:o + T]), inc=True)
            p.free_region(nm, "act", ev)

        q_evac = {}
        p.use_region("act", "big")
        q_last_mm = 0
        for oc in range(NQ):
            sa, qq_, cd = wq_src(oc)
            r, wq_done, _ = ring_load(sa, q=qq_, cache_dst=cd)
            p.wait("pe", qq_, wq_done)
            nm = halfname[oc % 2]
            p.use_region("pe", nm)
            bank, off = halfslots[oc % 2]
            for dc in range(DC):
                last = mm(bank[:, off:off + T], slc(ring, r * 4096 + dc * 128, 128),
                          slc(xn, dc * T, T), dc == 0, dc == DC - 1, inc=(dc == DC - 1))
            q_last_mm = last
            ring_free[r] = ("pe", last)
            p.wait("act", "pe", last)
            q_evac[oc] = p.op("act", lambda e, b=bank, o=off, c=oc:
                              e.copy(slc(big, c * T, T), b[:, o:o + T]), inc=True)
            p.free_region(nm, "act", q_evac[oc])
        p.free_region("xn", "pe", q_last_mm)
        tap("q", big[:, 0:NQ * T], "act", q_evac[NQ - 1])
        tap("k", kcache[:, :], "act", p.cnt["act"])
        tap("v", vcache[:, :], "act", p.cnt["act"])
        if phases < 2:
            p.wait("sync", "pool", p.cnt["pool"])
            st = dma("sync", outh.rearrange("(dc p) t -> p dc t", p=128)[:, :, t0:t0 + T],
                     x1[:, :].rearrange("p (dc t) -> p dc t", t=T))
            p.free_region("x1", "sync", st)
            return st

        pairs = [(qsub, h) for qsub in range(4) for h in range(NQ)]
        NP = len(pairs)
        sbank = {0: pCD, 1: pEF, 2: pAB}
        sname = {0: ("pCD0", "pCD1"), 1: ("pEF0", "pEF1"), 2: ("pAB0", "pAB1")}
        s_done, mask_done, exp_done, pnorm_done, tp_done, copy_done, pv_done, oevac = ({} for _ in range(8))

        def jlo_of(n):
            qsub, _ = pairs[n]
            return max(0, 4 - (4 * i + qsub))

        def attn_exp(n):
            rho = n % 3
            rp = n % 2
            jlo = jlo_of(n)
            bank = sbank[rho]
            p.wait("act", "dve", mask_done[n])
            if n - 3 >= 0:
                p.wait("act", "dve", recip_done[n - 3])
            exp_done[n] = p.op("act", lambda e, b=bank, r=rho, q=rp, jl=jlo:
                               e.activation(Pb[:, q * 640 + jl * 128: q * 640 + 640],
                                            b[:, jl * 128:640], AF.Exp, scale=SCALE,
                                            accum_out=rs[:, r:r + 1]), inc=True)

        def attn_copy(n):
            rho = n % 3
            rp = n % 2
            jlo = jlo_of(n)
            view = sbank[rho][:, 0:1024].bitcast(BF16)
            p.wait("dve", "pe", tp_done[n])
            if n - 2 >= 0:
                p.wait("dve", "pe", pv_done[n - 2])
            copy_done[n] = p.op("dve", lambda e, v=view, q=rp, jl=jlo:
                                e.tensor_copy(PTb[:, q * 640 + jl * 128: q * 640 + 640],
                                              v[:, jl * 128:640]), inc=True)

        def attn_oevac(n):
            qsub, h = pairs[n]
            rho2 = n % 2
            p.wait("dve", "pe", pv_done[n])
            oevac[n] = p.op("dve", lambda e, r=rho2, hh=h, q=qsub:
                            e.tensor_copy(big[:, hh * T + q * 128: hh * T + q * 128 + 128],
                                          pO[:, r * 512: r * 512 + 128]), inc=True)

        recip_done = {}

        def attn_recip(n):
            rho = n % 3
            p.wait("dve", "act", exp_done[n])
            recip_done[n] = p.op("dve", lambda e, r=rho: e.reciprocal(rrs[:, r:r + 1], rs[:, r:r + 1]), inc=True)

        def attn_pnorm(n):
            rho = n % 3
            rp = n % 2
            jlo = jlo_of(n)
            p.wait("act", "dve", recip_done[n])
            if n - 2 >= 0:
                p.wait("act", "pe", tp_done[n - 2])
            pnorm_done[n] = p.op("act", lambda e, r=rho, q=rp, jl=jlo:
                                 e.activation(Pn[:, q * 640 + jl * 128: q * 640 + 640],
                                              Pb[:, q * 640 + jl * 128: q * 640 + 640],
                                              AF.Copy, scale=rrs[:, r:r + 1]), inc=True)

        def attn_mask(n):
            qsub, h = pairs[n]
            rho = n % 3
            qc = 4 * i + qsub
            bank = sbank[rho]
            p.wait("dve", "pe", s_done[n])
            if qc >= 4:
                p.op("dve", lambda e, b=bank: e.tensor_add(b[:, 0:128], b[:, 0:128], maskb[:, 0:128]))
            mask_done[n] = p.op("dve", lambda e, b=bank:
                                e.tensor_add(b[:, 512:640], b[:, 512:640], maskb[:, 128:256]), inc=True)

        def attn_S(n):
            qsub, h = pairs[n]
            rho = n % 3
            qc = 4 * i + qsub
            jlo = jlo_of(n)
            hk = h // GQ
            bank = sbank[rho]
            p.wait("pe", "act", q_evac[h])
            if n - 3 >= 0:
                p.wait("pe", "dve", copy_done[n - 3])
            else:
                p.use_region("pe", *sname[rho])
            for j in range(jlo, 5):
                kc = qc - 4 + j
                half_t = (kc // 4) % 2
                c0 = hk * 1024 + half_t * T + (kc % 4) * 128
                last = mm(bank[:, j * 128:j * 128 + 128],
                          big[:, h * T + qsub * 128: h * T + qsub * 128 + 128],
                          kcache[:, c0:c0 + 128], True, True, inc=(j == 4))
            s_done[n] = last

        def attn_T(n):
            rho = n % 3
            rp = n % 2
            jlo = jlo_of(n)
            view = sbank[rho][:, 0:1024].bitcast(BF16)
            p.wait("pe", "act", pnorm_done[n])
            for j in range(jlo, 5):
                last = p.op("pe", lambda e, v=view, jj=j, q=rp:
                            e.transpose(v[:, jj * 128:jj * 128 + 128],
                                        Pn[:, q * 640 + jj * 128: q * 640 + jj * 128 + 128],
                                        idb[:, :]), inc=(j == 4))
            tp_done[n] = last

        def attn_PV(n):
            qsub, h = pairs[n]
            rho2 = n % 2
            rho = n % 3
            qc = 4 * i + qsub
            jlo = jlo_of(n)
            hk = h // GQ
            p.wait("pe", "dve", copy_done[n])
            if n - 2 >= 0:
                p.wait("pe", "dve", oevac[n - 2])
            else:
                p.use_region("pe", halfname[6 + rho2])
            for j in range(jlo, 5):
                kc = qc - 4 + j
                last = mm(pO[:, rho2 * 512: rho2 * 512 + 128],
                          vcache[:, (kc % 8) * 1024 + hk * 128: (kc % 8) * 1024 + hk * 128 + 128],
                          PTb[:, rho2 * 640 + j * 128: rho2 * 640 + j * 128 + 128],
                          j == jlo, j == 4, inc=(j == 4))
            pv_done[n] = last

        for n in range(NP + 5):
            if 0 <= n - 1 < NP:
                attn_exp(n - 1)
            if 0 <= n - 2 < NP:
                attn_pnorm(n - 2)
            if 0 <= n - 3 < NP:
                attn_copy(n - 3)
            if 0 <= n - 4 < NP:
                attn_oevac(n - 4)
            if 0 <= n - 1 < NP:
                attn_recip(n - 1)
            if n < NP:
                attn_S(n)
                attn_mask(n)
            if 0 <= n - 2 < NP:
                attn_T(n - 2)
            if 0 <= n - 3 < NP:
                attn_PV(n - 3)
        p.free_region("pO0", "dve", oevac[NP - 2])
        p.free_region("pO1", "dve", oevac[NP - 1])
        for b, names in sname.items():
            lastn = max(n for n in range(NP) if n % 3 == b)
            for nm_ in names:
                p.free_region(nm_, "dve", copy_done[lastn])
        p.free_region("kcache", "pe", s_done[NP - 1])
        p.free_region("vcache", "pe", pv_done[NP - 1])

        wo_flush = {}
        for dc in range(DC):
            sa, qq_, cd = wo_src(dc)
            r, wo_done, _ = ring_load(sa, q=qq_, cache_dst=cd)
            p.wait("pe", qq_, wo_done)
            p.wait("pe", "dve", oevac[NP - 1])
            nm = halfname[dc % 2]
            p.use_region("pe", nm)
            bank, off = halfslots[dc % 2]
            for oc in range(NQ):
                last = mm(bank[:, off:off + T], slc(ring, r * 4096 + oc * 128, 128),
                          slc(big, oc * T, T), oc == 0, oc == NQ - 1, inc=(oc == NQ - 1))
            ring_free[r] = ("pe", last)
            p.wait("dve", "pe", last)
            wo_flush[dc] = p.op("dve", lambda e, d=dc, b=bank, o=off:
                                e.tensor_add(slc(x1, d * T, T), slc(x1, d * T, T), b[:, o:o + T]), inc=True)
            p.free_region(nm, "dve", wo_flush[dc])
        wo_mm_done = p.cnt["pe"]
        p.free_region("big_pe", "pe", wo_mm_done)
        tap("ot", big[:, 0:NQ * T], "dve", oevac[NP - 1])
        tap("x1a", x1[:, :], "dve", wo_flush[DC - 1])
        if phases < 3:
            p.wait("sync", "dve", wo_flush[DC - 1])
            p.wait("sync", "pool", p.cnt["pool"])
            st = dma("sync", outh.rearrange("(dc p) t -> p dc t", p=128)[:, :, t0:t0 + T],
                     x1[:, :].rearrange("p (dc t) -> p dc t", t=T))
            p.free_region("x1", "sync", st)
            return st

        norm_done2 = ln({dc: ("dve", wo_flush[dc]) for dc in range(DC)})

        tap("y", xn[:, :], "dve", norm_done2[DC - 1])
        if phases < 4:
            p.wait("sync", "dve", norm_done2[DC - 1])
            p.wait("sync", "pool", p.cnt["pool"])
            st = dma("sync", outh.rearrange("(dc p) t -> p dc t", p=128)[:, :, t0:t0 + T],
                     x1[:, :].rearrange("p (dc t) -> p dc t", t=T))
            p.free_region("x1", "sync", st)
            return st
        silu_done, stt_done, gate_done, up_done = {}, {}, {}, {}
        down_mm, down_flush = {}, {}
        wd_wb_prev = [None]
        for g, ics in enumerate(GROUPS):
            Gg = len(ics)
            p.use_region("pool", "big_pe")
            if g > 0:
                p.wait("pool", "pe", down_mm[(g - 1, DC - 1)])
            if wd_wb_prev[0] is not None:
                p.wait("pool", "pool", wd_wb_prev[0])
            wd_src = wdh if i == 0 else wd_c
            wd_dma = dma("pool", big[:, 0:Gg * 4096].rearrange("p (g f) -> p g f", f=4096),
                         wd_src.rearrange("ic p f -> p ic f")[:, ics[0]:ics[0] + Gg, :])
            if i == 0:
                p.wait("pool", "pool", wd_dma)
                wd_wb_prev[0] = dma("pool", wd_c.rearrange("ic p f -> p ic f")[:, ics[0]:ics[0] + Gg, :],
                                    big[:, 0:Gg * 4096].rearrange("p (g f) -> p g f", f=4096))
            else:
                wd_wb_prev[0] = None
            for icg, ic in enumerate(ics):
                sa, qg_, cd = wg_src(ic)
                rg, wg_done, _ = ring_load(sa, q=qg_, cache_dst=cd)
                sa, qu_, cd = wu_src(ic)
                ru, wu_done, _ = ring_load(sa, q=qu_, cache_dst=cd)
                gi = 0 if ic % 2 == 0 else 4
                ui = 1 if ic % 2 == 0 else 5
                gbank, goff = halfslots[gi]
                ubank, uoff = halfslots[ui]
                p.wait("pe", qg_, wg_done)
                p.use_region("pe", halfname[gi])
                for dc in range(DC):
                    p.wait("pe", "dve", norm_done2[dc])
                    last = mm(gbank[:, goff:goff + T], slc(ring, rg * 4096 + dc * 128, 128),
                              slc(xn, dc * T, T), dc == 0, dc == DC - 1, inc=(dc == DC - 1))
                gate_done[ic] = last
                ring_free[rg] = ("pe", last)
                p.wait("pe", qu_, wu_done)
                p.use_region("pe", halfname[ui])
                for dc in range(DC):
                    last = mm(ubank[:, uoff:uoff + T], slc(ring, ru * 4096 + dc * 128, 128),
                              slc(xn, dc * T, T), dc == 0, dc == DC - 1, inc=(dc == DC - 1))
                up_done[ic] = last
                ring_free[ru] = ("pe", last)
                p.wait("act", "pe", gate_done[ic])
                p.use_region("act", "hsb")
                silu_done[ic] = p.op("act", lambda e, b=gbank, o=goff, c=ic:
                                     e.activation(hsb[:, :], b[:, o:o + T],
                                                  AF.Silu, scale=agb[:, c:c + 1]), inc=True)
                p.free_region(halfname[gi], "act", silu_done[ic])
                p.wait("dve", "pe", up_done[ic])
                p.wait("dve", "act", silu_done[ic])
                p.use_region("dve", "hgrp_pe")
                stt_done[ic] = p.op("dve", lambda e, b=ubank, o=uoff, c=ic, k=icg:
                                    e.scalar_tensor_tensor(slc(hgrp, k * T, T), b[:, o:o + T],
                                                           aub[:, c:c + 1], hsb[:, :],
                                                           ALU.mult, ALU.mult), inc=True)
                p.free_region("hsb", "dve", stt_done[ic])
                p.free_region(halfname[ui], "dve", stt_done[ic])
            p.wait("pe", "pool", wd_dma)
            p.wait("pe", "dve", stt_done[ics[-1]])
            for dc in range(DC):
                nm = "pCD0" if dc % 2 == 0 else "pCD1"
                p.use_region("pe", nm)
                bank, off = halfslots[2 + dc % 2]
                for icg in range(Gg):
                    last = mm(bank[:, off:off + T],
                              big[:, icg * 4096 + dc * 128: icg * 4096 + dc * 128 + 128],
                              slc(hgrp, icg * T, T), icg == 0, icg == Gg - 1, inc=(icg == Gg - 1))
                down_mm[(g, dc)] = last
                p.wait("dve", "pe", last)
                fl = p.op("dve", lambda e, d=dc, b=bank, o=off:
                          e.scalar_tensor_tensor(slc(x1, d * T, T), b[:, o:o + T],
                                                 adb[:, d:d + 1], slc(x1, d * T, T),
                                                 ALU.mult, ALU.add), inc=True)
                down_flush[(g, dc)] = fl
                p.free_region(nm, "dve", fl)
            p.free_region("hgrp_pe", "pe", down_mm[(g, DC - 1)])
        ng = len(GROUPS)
        p.free_region("big_pe", "pe", down_mm[(ng - 1, DC - 1)])
        p.free_region("big", "pe", down_mm[(ng - 1, DC - 1)])
        p.free_region("xn", "pe", up_done[GROUPS[-1][-1]])

        p.wait("sync", "dve", down_flush[(ng - 1, DC - 1)])
        st = dma("sync", outh.rearrange("(dc p) t -> p dc t", p=128)[:, :, t0:t0 + T],
                 x1[:, :].rearrange("p (dc t) -> p dc t", t=T))
        p.free_region("x1", "sync", st)
        return st

    last_store = None
    for i in range(nt):
        last_store = emit_tile(i, phases[i] if isinstance(phases, (list, tuple)) else phases)

    p.wait("sync", "sync", last_store)
    p.verify()
    with ctx:
        p.materialize(nc, sems)
    return nc


def _host_layout(L0_attn_x, wq, wk, wv, wo, Wg, Wu, Wd, ag, au, ad):
    def lhs_tiles(W, nko, nkd):
        t = W.reshape(nko, 128, nkd, 128).transpose(0, 3, 2, 1)
        return np.ascontiguousarray(t).reshape(nko, 128, nkd * 128)

    wq_l = lhs_tiles(wq, NQ, DC)
    wk_l = lhs_tiles(wk, NKV, DC)
    wo_l = lhs_tiles(wo, DC, NQ)
    wg_l = lhs_tiles(Wg, NIC, DC)
    wu_l = lhs_tiles(Wu, NIC, DC)
    wd_l = np.ascontiguousarray(Wd.reshape(DC, 128, NIC, 128).transpose(2, 3, 0, 1)).reshape(NIC, 128, D)
    wv_l = np.ascontiguousarray(wv.T).reshape(DC, 128, NKV * HD)
    ag_l = np.ascontiguousarray(ag.reshape(NIC, 128).T)
    au_l = np.ascontiguousarray(au.reshape(NIC, 128).T)
    ad_l = np.ascontiguousarray(ad.reshape(DC, 128).T)

    ii = np.arange(128)[:, None]
    jj = np.arange(128)[None, :]
    mask = np.empty((128, 256), dtype=np.float32)
    mask[:, 0:128] = np.where(jj > ii, 0.0, NEG)
    mask[:, 128:256] = np.where(jj <= ii, 0.0, NEG)
    idm = np.eye(128, dtype=ml_dtypes.bfloat16)
    onb = np.ones((128, 1), dtype=ml_dtypes.bfloat16)
    onf = np.ones((1, 128), dtype=np.float32)

    shared = dict(wq=wq_l, wk=wk_l, wv=wv_l, wo=wo_l, wg=wg_l, wu=wu_l, wd=wd_l,
                  ag=ag_l, au=au_l, ad=ad_l, mask=mask, idm=idm, onb=onb, onf=onf)
    per_core = []
    for b in range(8):
        m = dict(shared)
        m["x"] = np.ascontiguousarray(L0_attn_x[b].T)
        per_core.append(m)
    return per_core


def kernel(L0_attn_x, L0_attn_wq, L0_attn_wk, L0_attn_wv, L0_attn_wo,
           L0_ffn_Wg, L0_ffn_Wu, L0_ffn_Wd, L0_ffn_ag, L0_ffn_au, L0_ffn_ad):
    in_maps = _host_layout(np.asarray(L0_attn_x, dtype=np.float32),
                           np.asarray(L0_attn_wq, dtype=np.float32),
                           np.asarray(L0_attn_wk, dtype=np.float32),
                           np.asarray(L0_attn_wv, dtype=np.float32),
                           np.asarray(L0_attn_wo, dtype=np.float32),
                           np.asarray(L0_ffn_Wg, dtype=np.float32),
                           np.asarray(L0_ffn_Wu, dtype=np.float32),
                           np.asarray(L0_ffn_Wd, dtype=np.float32),
                           np.asarray(L0_ffn_ag, dtype=np.float32),
                           np.asarray(L0_ffn_au, dtype=np.float32),
                           np.asarray(L0_ffn_ad, dtype=np.float32))
    nc = build_program()
    res = run_bass_kernel_spmd(nc, in_maps, core_ids=list(range(8)))
    out = np.empty((8, SEQ, D), dtype=np.float32)
    for b in range(8):
        out[b] = res.results[b]["out"].T
    return out


# version 05 (final)
# speedup vs baseline: 1.1312x; 1.1312x over previous
import math
import numpy as np
import ml_dtypes
import concourse.bass as bass
import concourse.mybir as mybir
from concourse.bass_utils import run_bass_kernel_spmd

F32 = mybir.dt.float32
BF16 = mybir.dt.bfloat16
AF = mybir.ActivationFunctionType
ALU = mybir.AluOpType
AX = mybir.AxisListType

D = 4096
SEQ = 2048
T = 512
NT = SEQ // T
DC = D // 128
NQ, NKV, HD = 32, 8, 128
GQ = NQ // NKV
INTER = 11008
NIC = INTER // 128
G = 4
GROUPS = [list(range(s, min(s + G, NIC))) for s in range(0, NIC, G)]
SCALE = 1.0 / math.sqrt(HD)
NEG = -1.0e30
ENGINES = ("pe", "act", "dve", "sync", "pool")


class Prog:

    def __init__(self):
        self.ops = {e: [] for e in ENGINES}
        self.cnt = {e: 0 for e in ENGINES}
        self.waited = {e: {s: 0 for s in ENGINES} for e in ENGINES}
        self.region = {}

    def wait(self, eng, sem, val):
        if val is None or val <= 0:
            return
        if self.waited[eng][sem] >= val:
            return
        self.waited[eng][sem] = val
        self.ops[eng].append(("w", sem, val))

    def op(self, eng, fn, inc=False, n=1):
        if inc:
            self.cnt[eng] += n
        self.ops[eng].append(("o", fn, inc, n))
        return self.cnt[eng]

    def use_region(self, eng, *names):
        for nm in names:
            ev = self.region.get(nm)
            if ev is not None:
                self.wait(eng, ev[0], ev[1])

    def free_region(self, nm, sem, cnt):
        self.region[nm] = (sem, cnt)

    def verify(self):
        pos = {e: 0 for e in ENGINES}
        sim = {e: 0 for e in ENGINES}
        progress = True
        while progress:
            progress = False
            for e in ENGINES:
                while pos[e] < len(self.ops[e]):
                    rec = self.ops[e][pos[e]]
                    if rec[0] == "w" and sim[rec[1]] < rec[2]:
                        break
                    if rec[0] == "o" and rec[2]:
                        sim[e] += rec[3]
                    pos[e] += 1
                    progress = True
        stuck = {}
        for e in ENGINES:
            if pos[e] < len(self.ops[e]):
                rec = self.ops[e][pos[e]]
                stuck[e] = f"at {pos[e]}/{len(self.ops[e])} waiting {rec[1]}>={rec[2]} (have {sim[rec[1]]})"
        if stuck:
            raise RuntimeError(f"schedule deadlock: {stuck}")

    def materialize(self, nc, sems):
        with nc.Block() as block:
            def runner(eng_name):
                def run(e):
                    for rec in self.ops[eng_name]:
                        if rec[0] == "w":
                            e.wait_ge(sems[rec[1]], rec[2])
                        else:
                            ins = rec[1](e)
                            if rec[2]:
                                ins.then_inc(sems[eng_name], rec[3])
                return run

            block.tensor(runner("pe"))
            block.scalar(runner("act"))
            block.vector(runner("dve"))
            block.sync(runner("sync"))
            block.gpsimd(runner("pool"))


def build_program(nt=NT, phases=7, taps=()):
    nc = bass.Bass("TRN2", target_bir_lowering=False, dynamic_dma_scratch_size=16384)

    xh = nc.dram_tensor("x", [D, SEQ], F32, kind="ExternalInput").ap()
    wqh = nc.dram_tensor("wq", [NQ, 128, D], F32, kind="ExternalInput").ap()
    wkh = nc.dram_tensor("wk", [NKV, 128, D], F32, kind="ExternalInput").ap()
    wvh = nc.dram_tensor("wv", [DC, 128, NKV * HD], F32, kind="ExternalInput").ap()
    woh = nc.dram_tensor("wo", [DC, 128, D], F32, kind="ExternalInput").ap()
    wgh = nc.dram_tensor("wg", [NIC, 128, D], F32, kind="ExternalInput").ap()
    wuh = nc.dram_tensor("wu", [NIC, 128, D], F32, kind="ExternalInput").ap()
    wdh = nc.dram_tensor("wd", [NIC, 128, D], F32, kind="ExternalInput").ap()
    agh = nc.dram_tensor("ag", [128, NIC], F32, kind="ExternalInput").ap()
    auh = nc.dram_tensor("au", [128, NIC], F32, kind="ExternalInput").ap()
    adh = nc.dram_tensor("ad", [128, DC], F32, kind="ExternalInput").ap()
    maskh = nc.dram_tensor("mask", [128, 256], F32, kind="ExternalInput").ap()
    idh = nc.dram_tensor("idm", [128, 128], BF16, kind="ExternalInput").ap()
    onbh = nc.dram_tensor("onb", [128, 1], BF16, kind="ExternalInput").ap()
    onfh = nc.dram_tensor("onf", [1, 128], F32, kind="ExternalInput").ap()
    outh = nc.dram_tensor("out", [D, SEQ], F32, kind="ExternalOutput").ap()
    wq_c = nc.dram_tensor("wq_c", [NQ, 128, D], BF16, kind="Internal").ap()
    wk_c = nc.dram_tensor("wk_c", [NKV, 128, D], BF16, kind="Internal").ap()
    wv_c = nc.dram_tensor("wv_c", [DC, 128, NKV * HD], BF16, kind="Internal").ap()
    wo_c = nc.dram_tensor("wo_c", [DC, 128, D], BF16, kind="Internal").ap()
    wg_c = nc.dram_tensor("wg_c", [NIC, 128, D], BF16, kind="Internal").ap()
    wu_c = nc.dram_tensor("wu_c", [NIC, 128, D], BF16, kind="Internal").ap()
    wd_c = nc.dram_tensor("wd_c", [NIC, 128, D], BF16, kind="Internal").ap()
    tap_out = {}
    for tname, tshape in taps:
        tap_out[tname] = nc.dram_tensor("tap_" + tname, tshape, F32, kind="ExternalOutput").ap()

    import contextlib
    ctx = contextlib.ExitStack()
    sb = lambda name, shape, dt: ctx.enter_context(nc.sbuf_tensor(name, shape, dt))
    ps = lambda name, shape, dt: ctx.enter_context(nc.psum_tensor(name, shape, dt))

    x1 = sb("x1", [128, DC * T], F32)
    xn = sb("xn", [128, DC * T], BF16)
    big = sb("big", [128, max(NQ * T, G * 4096)], BF16)
    kcache = sb("kcache", [128, NKV * 1024], BF16)
    vcache = sb("vcache", [128, 8 * NKV * HD], BF16)
    ring = sb("ring", [128, 3 * 4096], BF16)
    hgrp = sb("hgrp", [128, G * T], BF16)
    Pb = sb("Pb", [128, 2 * 640], BF16)
    PTb = sb("PTb", [128, 2 * 640], BF16)
    Pn = sb("Pn", [128, 2 * 640], BF16)
    hsb = sb("hsb", [128, T], BF16)
    sqb = sb("sqb", [128, T], BF16)
    tmpbuf = sb("tmpbuf", [128, 6], F32)
    strow = sb("strow", [1, 2 * T], F32)
    maskb = sb("maskb", [128, 256], F32)
    rstdb = sb("rstdb", [128, T], F32)
    mrb = sb("mrb", [128, T], BF16)
    idb = sb("idb", [128, 128], BF16)
    onb = sb("onb_s", [128, 1], BF16)
    onf = sb("onf_s", [1, 128], F32)
    scales = sb("scales", [128, 2 * NIC + DC], F32)
    agb = scales[:, 0:NIC]
    aub = scales[:, NIC:2 * NIC]
    adb = scales[:, 2 * NIC:2 * NIC + DC]
    rs = tmpbuf[:, 0:3]
    rrs = tmpbuf[:, 3:6]
    tmp = Pb[:, 0:1024].bitcast(F32)[:, 0:T]
    epsb = sb("epsb", [1, 1], F32)

    pAB = ps("pAB", [128, 1024], F32)
    pCD = ps("pCD", [128, 1024], F32)
    pEF = ps("pEF", [128, 1024], F32)
    pO = ps("pO", [128, 1024], F32)

    sems = {e: ctx.enter_context(nc.semaphore(f"sem_{e}")) for e in ENGINES}

    p = Prog()

    def dma(eng, out, in_):
        return p.op(eng, lambda e, o=out, i=in_: e.dma_start(o, i), inc=True, n=16)

    def mm(out, lhsT, rhs, start, stop, inc=False):
        return p.op("pe", lambda e, o=out, l=lhsT, r=rhs, st=start, sp=stop:
                    e.matmul(o, l, r, start=st, stop=sp), inc=inc)

    def slc(t, c0, n):
        return t[:, c0:c0 + n]

    halfslots = [(pAB, 0), (pAB, 512), (pCD, 0), (pCD, 512), (pEF, 0), (pEF, 512), (pO, 0), (pO, 512)]
    halfname = ["pAB0", "pAB1", "pCD0", "pCD1", "pEF0", "pEF1", "pO0", "pO1"]

    for dst, src in ((maskb[:, :], maskh), (idb[:, :], idh), (onb[:, :], onbh), (onf[:, :], onfh),
                     (agb, agh), (aub, auh), (adb, adh)):
        dma("sync", dst, src)
    cst = p.cnt["sync"]
    for e in ("pe", "dve", "act", "pool"):
        p.wait(e, "sync", cst)
    eps_done = p.op("dve", lambda e: e.memset(epsb[:, :], 1.0e-5), inc=True)
    p.wait("act", "dve", eps_done)

    ring_free = [None, None, None]
    ring_next = [0]

    def ring_load(src_ap, ncols=4096, q="pool", cache_dst=None):
        r = ring_next[0]
        ring_next[0] = (r + 1) % 3
        if ring_free[r] is not None:
            p.wait(q, ring_free[r][0], ring_free[r][1])
        done = dma(q, slc(ring, r * 4096, ncols), src_ap)
        if cache_dst is not None:
            p.wait("sync", q, done)
            wb = dma("sync", cache_dst, slc(ring, r * 4096, ncols))
            ring_free[r] = ("sync", wb)
            return r, done, ("sync", wb)
        return r, done, None

    def ln(x1_ready):
        p.use_region("act", "xn")
        st_done = 0
        for dc in range(DC):
            if x1_ready is not None:
                p.wait("act", x1_ready[dc][0], x1_ready[dc][1])
            cast_done = p.op("act", lambda e, d=dc: e.copy(slc(xn, d * T, T), slc(x1, d * T, T)), inc=True)
            p.use_region("dve", "sqb")
            p.wait("dve", "act", cast_done)
            sq_done = p.op("dve", lambda e, d=dc: e.tensor_mul(sqb[:, :], slc(xn, d * T, T), slc(xn, d * T, T)), inc=True)
            p.wait("pe", "act", cast_done)
            if dc == 0:
                p.use_region("pe", "pAB0", "pAB1")
            mm(pAB[0:1, 0:T], onb[:, 0:1], slc(xn, dc * T, T), dc == 0, dc == DC - 1)
            p.wait("pe", "dve", sq_done)
            st_done = mm(pAB[0:1, 512:512 + T], onb[:, 0:1], sqb[:, :], dc == 0, dc == DC - 1, inc=True)
            p.free_region("sqb", "pe", st_done)
        p.wait("dve", "pe", st_done)
        p.use_region("dve", "strow")
        r0, r1 = strow[0:1, 0:T], strow[0:1, T:2 * T]
        rt = tmp[0:1, 0:T]
        p.op("dve", lambda e: e.tensor_scalar_mul(r0, pAB[0:1, 0:T], 1.0 / D))
        fr0 = p.op("dve", lambda e: e.tensor_scalar_mul(r1, pAB[0:1, 512:512 + T], 1.0 / D), inc=True)
        p.free_region("pAB0", "dve", fr0)
        p.free_region("pAB1", "dve", fr0)
        p.op("dve", lambda e: e.tensor_mul(rt, r0, r0))
        var_done = p.op("dve", lambda e: e.tensor_sub(r1, r1, rt), inc=True)
        p.wait("act", "dve", var_done)
        p.op("act", lambda e: e.activation(r1, r1, AF.Ln, bias=epsb[0:1, 0:1]))
        rstd_done = p.op("act", lambda e: e.activation(r1, r1, AF.Exp, scale=-0.5), inc=True)
        p.wait("dve", "act", rstd_done)
        mr_done = p.op("dve", lambda e: e.tensor_mul(r0, r0, r1), inc=True)
        p.wait("pe", "dve", mr_done)
        p.use_region("pe", "pCD0", "pCD1")
        mm(pCD[:, 0:T], onf[0:1, :], r1, True, True)
        bc_done = mm(pCD[:, 512:512 + T], onf[0:1, :], r0, True, True, inc=True)
        p.free_region("strow", "pe", bc_done)
        p.wait("dve", "pe", bc_done)
        p.op("dve", lambda e: e.tensor_copy(rstdb[:, :], pCD[:, 0:T]))
        ev_done = p.op("dve", lambda e: e.tensor_copy(mrb[:, :], pCD[:, 512:512 + T]), inc=True)
        p.free_region("pCD0", "dve", ev_done)
        p.free_region("pCD1", "dve", ev_done)
        norm_done = {}
        for dc in range(DC):
            p.op("dve", lambda e, d=dc: e.tensor_mul(tmp, slc(x1, d * T, T), rstdb[:, :]))
            norm_done[dc] = p.op("dve", lambda e, d=dc:
                                 e.tensor_sub(slc(xn, d * T, T), tmp, mrb[:, :]), inc=True)
        return norm_done

    def emit_tile(i, phases):
        t0 = i * T
        par = i % 2

        p.use_region("sync", "x1")
        xload = dma("sync", x1[:, :].rearrange("p (dc t) -> p dc t", t=T),
                    xh.rearrange("(dc p) t -> p dc t", p=128)[:, :, t0:t0 + T])

        norm_done = ln({dc: ("sync", xload) for dc in range(DC)})

        def tap(name, ap_src, sem, cnt):
            if name not in tap_out:
                return 0
            p.wait("pool", sem, cnt)
            return dma("pool", tap_out[name], ap_src)

        tap("xn1", xn[:, :], "dve", norm_done[DC - 1])

        wq_src = (lambda c: (wqh[c], "pool", wq_c[c])) if i == 0 else (lambda c: (wq_c[c], "sync", None))
        wk_src = (lambda c: (wkh[c], "pool", wk_c[c])) if i == 0 else (lambda c: (wk_c[c], "sync", None))
        wv_src = (lambda c: (wvh[c], "pool", wv_c[c])) if i == 0 else (lambda c: (wv_c[c], "sync", None))
        wo_src = (lambda c: (woh[c], "pool", wo_c[c])) if i == 0 else (lambda c: (wo_c[c], "sync", None))
        wg_src = (lambda c: (wgh[c], "pool", wg_c[c])) if i <= 1 else (lambda c: (wg_c[c], "sync", None))
        wu_src = (lambda c: (wuh[c], "pool", wu_c[c])) if i <= 1 else (lambda c: (wu_c[c], "sync", None))
        if i == 1:
            pass
        for dc in range(DC):
            sa, qq_, cd = wv_src(dc)
            r, wv_done, _ = ring_load(sa, ncols=NKV * HD, q=qq_, cache_dst=cd)
            p.wait("pe", qq_, wv_done)
            p.wait("pe", "dve", norm_done[dc])
            if dc == 0:
                p.use_region("pe", *halfname)
            for tc in range(4):
                for half in range(2):
                    bank, off = halfslots[tc * 2 + half]
                    last = mm(bank[:, off:off + T],
                              xn[:, dc * T + tc * 128: dc * T + tc * 128 + 128],
                              slc(ring, r * 4096 + half * T, T),
                              dc == 0, dc == DC - 1,
                              inc=(dc == DC - 1 or (tc == 3 and half == 1)))
            ring_free[r] = ("pe", last)
        vmm_done = p.cnt["pe"]
        for s8 in range(8):
            tc, half = s8 // 2, s8 % 2
            bank, off = halfslots[s8]
            vslot = par * 4 + tc
            p.wait("act", "pe", vmm_done)
            p.use_region("act", "vcache")
            ev = p.op("act", lambda e, b=bank, o=off, vs=vslot, hf=half:
                      e.copy(vcache[:, vs * 1024 + hf * T: vs * 1024 + hf * T + T], b[:, o:o + T]), inc=True)
            p.free_region(halfname[s8], "act", ev)

        for hk in range(NKV):
            sa, qq_, cd = wk_src(hk)
            r, wk_done, _ = ring_load(sa, q=qq_, cache_dst=cd)
            p.wait("pe", qq_, wk_done)
            nm = halfname[hk % 2]
            p.use_region("pe", nm)
            bank, off = halfslots[hk % 2]
            for dc in range(DC):
                last = mm(bank[:, off:off + T], slc(ring, r * 4096 + dc * 128, 128),
                          slc(xn, dc * T, T), dc == 0, dc == DC - 1, inc=(dc == DC - 1))
            ring_free[r] = ("pe", last)
            p.wait("act", "pe", last)
            p.use_region("act", "kcache")
            ev = p.op("act", lambda e, b=bank, o=off, h=hk:
                      e.copy(kcache[:, h * 1024 + par * T: h * 1024 + par * T + T], b[:, o:o + T]), inc=True)
            p.free_region(nm, "act", ev)

        q_evac = {}
        p.use_region("act", "big")
        q_last_mm = 0
        for oc in range(NQ):
            sa, qq_, cd = wq_src(oc)
            r, wq_done, _ = ring_load(sa, q=qq_, cache_dst=cd)
            p.wait("pe", qq_, wq_done)
            nm = halfname[oc % 2]
            p.use_region("pe", nm)
            bank, off = halfslots[oc % 2]
            for dc in range(DC):
                last = mm(bank[:, off:off + T], slc(ring, r * 4096 + dc * 128, 128),
                          slc(xn, dc * T, T), dc == 0, dc == DC - 1, inc=(dc == DC - 1))
            q_last_mm = last
            ring_free[r] = ("pe", last)
            p.wait("act", "pe", last)
            q_evac[oc] = p.op("act", lambda e, b=bank, o=off, c=oc:
                              e.copy(slc(big, c * T, T), b[:, o:o + T]), inc=True)
            p.free_region(nm, "act", q_evac[oc])
        p.free_region("xn", "pe", q_last_mm)
        tap("q", big[:, 0:NQ * T], "act", q_evac[NQ - 1])
        tap("k", kcache[:, :], "act", p.cnt["act"])
        tap("v", vcache[:, :], "act", p.cnt["act"])
        if phases < 2:
            p.wait("sync", "pool", p.cnt["pool"])
            st = dma("sync", outh.rearrange("(dc p) t -> p dc t", p=128)[:, :, t0:t0 + T],
                     x1[:, :].rearrange("p (dc t) -> p dc t", t=T))
            p.free_region("x1", "sync", st)
            return st

        pairs = [(qsub, h) for qsub in range(4) for h in range(NQ)]
        NP = len(pairs)
        sbank = {0: pCD, 1: pEF, 2: pAB}
        sname = {0: ("pCD0", "pCD1"), 1: ("pEF0", "pEF1"), 2: ("pAB0", "pAB1")}
        s_done, mask_done, exp_done, pnorm_done, tp_done, copy_done, pv_done, oevac = ({} for _ in range(8))

        def jlo_of(n):
            qsub, _ = pairs[n]
            return max(0, 4 - (4 * i + qsub))

        def attn_exp(n):
            rho = n % 3
            rp = n % 2
            jlo = jlo_of(n)
            bank = sbank[rho]
            p.wait("act", "dve", mask_done[n])
            if n - 3 >= 0:
                p.wait("act", "dve", recip_done[n - 3])
            exp_done[n] = p.op("act", lambda e, b=bank, r=rho, q=rp, jl=jlo:
                               e.activation(Pb[:, q * 640 + jl * 128: q * 640 + 640],
                                            b[:, jl * 128:640], AF.Exp, scale=SCALE,
                                            accum_out=rs[:, r:r + 1]), inc=True)

        def attn_copy(n):
            rho = n % 3
            rp = n % 2
            jlo = jlo_of(n)
            view = sbank[rho][:, 0:1024].bitcast(BF16)
            p.wait("dve", "pe", tp_done[n])
            if n - 2 >= 0:
                p.wait("dve", "pe", pv_done[n - 2])
            copy_done[n] = p.op("dve", lambda e, v=view, q=rp, jl=jlo:
                                e.tensor_copy(PTb[:, q * 640 + jl * 128: q * 640 + 640],
                                              v[:, jl * 128:640]), inc=True)

        def attn_oevac(n):
            qsub, h = pairs[n]
            rho2 = n % 2
            p.wait("dve", "pe", pv_done[n])
            oevac[n] = p.op("dve", lambda e, r=rho2, hh=h, q=qsub:
                            e.tensor_copy(big[:, hh * T + q * 128: hh * T + q * 128 + 128],
                                          pO[:, r * 512: r * 512 + 128]), inc=True)

        recip_done = {}

        def attn_recip(n):
            rho = n % 3
            p.wait("dve", "act", exp_done[n])
            recip_done[n] = p.op("dve", lambda e, r=rho: e.reciprocal(rrs[:, r:r + 1], rs[:, r:r + 1]), inc=True)

        def attn_pnorm(n):
            rho = n % 3
            rp = n % 2
            jlo = jlo_of(n)
            p.wait("act", "dve", recip_done[n])
            if n - 2 >= 0:
                p.wait("act", "pe", tp_done[n - 2])
            pnorm_done[n] = p.op("act", lambda e, r=rho, q=rp, jl=jlo:
                                 e.activation(Pn[:, q * 640 + jl * 128: q * 640 + 640],
                                              Pb[:, q * 640 + jl * 128: q * 640 + 640],
                                              AF.Copy, scale=rrs[:, r:r + 1]), inc=True)

        def attn_mask(n):
            qsub, h = pairs[n]
            rho = n % 3
            qc = 4 * i + qsub
            bank = sbank[rho]
            p.wait("dve", "pe", s_done[n])
            if qc >= 4:
                p.op("dve", lambda e, b=bank: e.tensor_add(b[:, 0:128], b[:, 0:128], maskb[:, 0:128]))
            mask_done[n] = p.op("dve", lambda e, b=bank:
                                e.tensor_add(b[:, 512:640], b[:, 512:640], maskb[:, 128:256]), inc=True)

        def attn_S(n):
            qsub, h = pairs[n]
            rho = n % 3
            qc = 4 * i + qsub
            jlo = jlo_of(n)
            hk = h // GQ
            bank = sbank[rho]
            p.wait("pe", "act", q_evac[h])
            if n - 3 >= 0:
                p.wait("pe", "dve", copy_done[n - 3])
            else:
                p.use_region("pe", *sname[rho])
            for j in range(jlo, 5):
                kc = qc - 4 + j
                half_t = (kc // 4) % 2
                c0 = hk * 1024 + half_t * T + (kc % 4) * 128
                last = mm(bank[:, j * 128:j * 128 + 128],
                          big[:, h * T + qsub * 128: h * T + qsub * 128 + 128],
                          kcache[:, c0:c0 + 128], True, True, inc=(j == 4))
            s_done[n] = last

        def attn_T(n):
            rho = n % 3
            rp = n % 2
            jlo = jlo_of(n)
            view = sbank[rho][:, 0:1024].bitcast(BF16)
            p.wait("pe", "act", pnorm_done[n])
            for j in range(jlo, 5):
                last = p.op("pe", lambda e, v=view, jj=j, q=rp:
                            e.transpose(v[:, jj * 128:jj * 128 + 128],
                                        Pn[:, q * 640 + jj * 128: q * 640 + jj * 128 + 128],
                                        idb[:, :]), inc=(j == 4))
            tp_done[n] = last

        def attn_PV(n):
            qsub, h = pairs[n]
            rho2 = n % 2
            rho = n % 3
            qc = 4 * i + qsub
            jlo = jlo_of(n)
            hk = h // GQ
            p.wait("pe", "dve", copy_done[n])
            if n - 2 >= 0:
                p.wait("pe", "dve", oevac[n - 2])
            else:
                p.use_region("pe", halfname[6 + rho2])
            for j in range(jlo, 5):
                kc = qc - 4 + j
                last = mm(pO[:, rho2 * 512: rho2 * 512 + 128],
                          vcache[:, (kc % 8) * 1024 + hk * 128: (kc % 8) * 1024 + hk * 128 + 128],
                          PTb[:, rho2 * 640 + j * 128: rho2 * 640 + j * 128 + 128],
                          j == jlo, j == 4, inc=(j == 4))
            pv_done[n] = last

        for n in range(NP + 5):
            if 0 <= n - 1 < NP:
                attn_exp(n - 1)
            if 0 <= n - 2 < NP:
                attn_pnorm(n - 2)
            if 0 <= n - 3 < NP:
                attn_copy(n - 3)
            if 0 <= n - 4 < NP:
                attn_oevac(n - 4)
            if 0 <= n - 1 < NP:
                attn_recip(n - 1)
            if n < NP:
                attn_S(n)
                attn_mask(n)
            if 0 <= n - 2 < NP:
                attn_T(n - 2)
            if 0 <= n - 3 < NP:
                attn_PV(n - 3)
        p.free_region("pO0", "dve", oevac[NP - 2])
        p.free_region("pO1", "dve", oevac[NP - 1])
        for b, names in sname.items():
            lastn = max(n for n in range(NP) if n % 3 == b)
            for nm_ in names:
                p.free_region(nm_, "dve", copy_done[lastn])
        p.free_region("kcache", "pe", s_done[NP - 1])
        p.free_region("vcache", "pe", pv_done[NP - 1])

        wo_flush = {}
        for dc in range(DC):
            sa, qq_, cd = wo_src(dc)
            r, wo_done, _ = ring_load(sa, q=qq_, cache_dst=cd)
            p.wait("pe", qq_, wo_done)
            p.wait("pe", "dve", oevac[NP - 1])
            nm = halfname[dc % 2]
            p.use_region("pe", nm)
            bank, off = halfslots[dc % 2]
            for oc in range(NQ):
                last = mm(bank[:, off:off + T], slc(ring, r * 4096 + oc * 128, 128),
                          slc(big, oc * T, T), oc == 0, oc == NQ - 1, inc=(oc == NQ - 1))
            ring_free[r] = ("pe", last)
            p.wait("dve", "pe", last)
            wo_flush[dc] = p.op("dve", lambda e, d=dc, b=bank, o=off:
                                e.tensor_add(slc(x1, d * T, T), slc(x1, d * T, T), b[:, o:o + T]), inc=True)
            p.free_region(nm, "dve", wo_flush[dc])
        wo_mm_done = p.cnt["pe"]
        p.free_region("big_pe", "pe", wo_mm_done)
        tap("ot", big[:, 0:NQ * T], "dve", oevac[NP - 1])
        tap("x1a", x1[:, :], "dve", wo_flush[DC - 1])
        if phases < 3:
            p.wait("sync", "dve", wo_flush[DC - 1])
            p.wait("sync", "pool", p.cnt["pool"])
            st = dma("sync", outh.rearrange("(dc p) t -> p dc t", p=128)[:, :, t0:t0 + T],
                     x1[:, :].rearrange("p (dc t) -> p dc t", t=T))
            p.free_region("x1", "sync", st)
            return st

        norm_done2 = ln({dc: ("dve", wo_flush[dc]) for dc in range(DC)})

        tap("y", xn[:, :], "dve", norm_done2[DC - 1])
        if phases < 4:
            p.wait("sync", "dve", norm_done2[DC - 1])
            p.wait("sync", "pool", p.cnt["pool"])
            st = dma("sync", outh.rearrange("(dc p) t -> p dc t", p=128)[:, :, t0:t0 + T],
                     x1[:, :].rearrange("p (dc t) -> p dc t", t=T))
            p.free_region("x1", "sync", st)
            return st
        silu_done, stt_done, gate_done, up_done = {}, {}, {}, {}
        down_mm, down_flush = {}, {}
        wd_wb_prev = [None]
        for g, ics in enumerate(GROUPS):
            Gg = len(ics)
            p.use_region("pool", "big_pe")
            if g > 0:
                p.wait("pool", "pe", down_mm[(g - 1, DC - 1)])
            if wd_wb_prev[0] is not None:
                p.wait("pool", "sync", wd_wb_prev[0])
            wd_src = wdh if i == 0 else wd_c
            wd_dma = dma("pool", big[:, 0:Gg * 4096].rearrange("p (g f) -> p g f", f=4096),
                         wd_src.rearrange("ic p f -> p ic f")[:, ics[0]:ics[0] + Gg, :])
            if i == 0:
                p.wait("sync", "pool", wd_dma)
                wd_wb_prev[0] = dma("sync", wd_c.rearrange("ic p f -> p ic f")[:, ics[0]:ics[0] + Gg, :],
                                    big[:, 0:Gg * 4096].rearrange("p (g f) -> p g f", f=4096))
            else:
                wd_wb_prev[0] = None
            for icg, ic in enumerate(ics):
                sa, qg_, cd = wg_src(ic)
                rg, wg_done, _ = ring_load(sa, q=qg_, cache_dst=cd)
                sa, qu_, cd = wu_src(ic)
                ru, wu_done, _ = ring_load(sa, q=qu_, cache_dst=cd)
                gi = 0 if ic % 2 == 0 else 4
                ui = 1 if ic % 2 == 0 else 5
                gbank, goff = halfslots[gi]
                ubank, uoff = halfslots[ui]
                p.wait("pe", qg_, wg_done)
                p.use_region("pe", halfname[gi])
                for dc in range(DC):
                    p.wait("pe", "dve", norm_done2[dc])
                    last = mm(gbank[:, goff:goff + T], slc(ring, rg * 4096 + dc * 128, 128),
                              slc(xn, dc * T, T), dc == 0, dc == DC - 1, inc=(dc == DC - 1))
                gate_done[ic] = last
                ring_free[rg] = ("pe", last)
                p.wait("pe", qu_, wu_done)
                p.use_region("pe", halfname[ui])
                for dc in range(DC):
                    last = mm(ubank[:, uoff:uoff + T], slc(ring, ru * 4096 + dc * 128, 128),
                              slc(xn, dc * T, T), dc == 0, dc == DC - 1, inc=(dc == DC - 1))
                up_done[ic] = last
                ring_free[ru] = ("pe", last)
                p.wait("act", "pe", gate_done[ic])
                p.use_region("act", "hsb")
                silu_done[ic] = p.op("act", lambda e, b=gbank, o=goff, c=ic:
                                     e.activation(hsb[:, :], b[:, o:o + T],
                                                  AF.Silu, scale=agb[:, c:c + 1]), inc=True)
                p.free_region(halfname[gi], "act", silu_done[ic])
                p.wait("dve", "pe", up_done[ic])
                p.wait("dve", "act", silu_done[ic])
                p.use_region("dve", "hgrp_pe")
                stt_done[ic] = p.op("dve", lambda e, b=ubank, o=uoff, c=ic, k=icg:
                                    e.scalar_tensor_tensor(slc(hgrp, k * T, T), b[:, o:o + T],
                                                           aub[:, c:c + 1], hsb[:, :],
                                                           ALU.mult, ALU.mult), inc=True)
                p.free_region("hsb", "dve", stt_done[ic])
                p.free_region(halfname[ui], "dve", stt_done[ic])
            p.wait("pe", "pool", wd_dma)
            p.wait("pe", "dve", stt_done[ics[-1]])
            for dc in range(DC):
                nm = "pCD0" if dc % 2 == 0 else "pCD1"
                p.use_region("pe", nm)
                bank, off = halfslots[2 + dc % 2]
                for icg in range(Gg):
                    last = mm(bank[:, off:off + T],
                              big[:, icg * 4096 + dc * 128: icg * 4096 + dc * 128 + 128],
                              slc(hgrp, icg * T, T), icg == 0, icg == Gg - 1, inc=(icg == Gg - 1))
                down_mm[(g, dc)] = last
                p.wait("dve", "pe", last)
                fl = p.op("dve", lambda e, d=dc, b=bank, o=off:
                          e.scalar_tensor_tensor(slc(x1, d * T, T), b[:, o:o + T],
                                                 adb[:, d:d + 1], slc(x1, d * T, T),
                                                 ALU.mult, ALU.add), inc=True)
                down_flush[(g, dc)] = fl
                p.free_region(nm, "dve", fl)
            p.free_region("hgrp_pe", "pe", down_mm[(g, DC - 1)])
        ng = len(GROUPS)
        p.free_region("big_pe", "pe", down_mm[(ng - 1, DC - 1)])
        p.free_region("big", "pe", down_mm[(ng - 1, DC - 1)])
        p.free_region("xn", "pe", up_done[GROUPS[-1][-1]])

        p.wait("sync", "dve", down_flush[(ng - 1, DC - 1)])
        st = dma("sync", outh.rearrange("(dc p) t -> p dc t", p=128)[:, :, t0:t0 + T],
                 x1[:, :].rearrange("p (dc t) -> p dc t", t=T))
        p.free_region("x1", "sync", st)
        return st

    last_store = None
    for i in range(nt):
        last_store = emit_tile(i, phases[i] if isinstance(phases, (list, tuple)) else phases)

    p.wait("sync", "sync", last_store)
    p.verify()
    with ctx:
        p.materialize(nc, sems)
    return nc


def _host_layout(L0_attn_x, wq, wk, wv, wo, Wg, Wu, Wd, ag, au, ad):
    def lhs_tiles(W, nko, nkd):
        t = W.reshape(nko, 128, nkd, 128).transpose(0, 3, 2, 1)
        return np.ascontiguousarray(t).reshape(nko, 128, nkd * 128)

    wq_l = lhs_tiles(wq, NQ, DC)
    wk_l = lhs_tiles(wk, NKV, DC)
    wo_l = lhs_tiles(wo, DC, NQ)
    wg_l = lhs_tiles(Wg, NIC, DC)
    wu_l = lhs_tiles(Wu, NIC, DC)
    wd_l = np.ascontiguousarray(Wd.reshape(DC, 128, NIC, 128).transpose(2, 3, 0, 1)).reshape(NIC, 128, D)
    wv_l = np.ascontiguousarray(wv.T).reshape(DC, 128, NKV * HD)
    ag_l = np.ascontiguousarray(ag.reshape(NIC, 128).T)
    au_l = np.ascontiguousarray(au.reshape(NIC, 128).T)
    ad_l = np.ascontiguousarray(ad.reshape(DC, 128).T)

    ii = np.arange(128)[:, None]
    jj = np.arange(128)[None, :]
    mask = np.empty((128, 256), dtype=np.float32)
    mask[:, 0:128] = np.where(jj > ii, 0.0, NEG)
    mask[:, 128:256] = np.where(jj <= ii, 0.0, NEG)
    idm = np.eye(128, dtype=ml_dtypes.bfloat16)
    onb = np.ones((128, 1), dtype=ml_dtypes.bfloat16)
    onf = np.ones((1, 128), dtype=np.float32)

    shared = dict(wq=wq_l, wk=wk_l, wv=wv_l, wo=wo_l, wg=wg_l, wu=wu_l, wd=wd_l,
                  ag=ag_l, au=au_l, ad=ad_l, mask=mask, idm=idm, onb=onb, onf=onf)
    per_core = []
    for b in range(8):
        m = dict(shared)
        m["x"] = np.ascontiguousarray(L0_attn_x[b].T)
        per_core.append(m)
    return per_core


def kernel(L0_attn_x, L0_attn_wq, L0_attn_wk, L0_attn_wv, L0_attn_wo,
           L0_ffn_Wg, L0_ffn_Wu, L0_ffn_Wd, L0_ffn_ag, L0_ffn_au, L0_ffn_ad):
    in_maps = _host_layout(np.asarray(L0_attn_x, dtype=np.float32),
                           np.asarray(L0_attn_wq, dtype=np.float32),
                           np.asarray(L0_attn_wk, dtype=np.float32),
                           np.asarray(L0_attn_wv, dtype=np.float32),
                           np.asarray(L0_attn_wo, dtype=np.float32),
                           np.asarray(L0_ffn_Wg, dtype=np.float32),
                           np.asarray(L0_ffn_Wu, dtype=np.float32),
                           np.asarray(L0_ffn_Wd, dtype=np.float32),
                           np.asarray(L0_ffn_ag, dtype=np.float32),
                           np.asarray(L0_ffn_au, dtype=np.float32),
                           np.asarray(L0_ffn_ad, dtype=np.float32))
    nc = build_program()
    res = run_bass_kernel_spmd(nc, in_maps, core_ids=list(range(8)))
    out = np.empty((8, SEQ, D), dtype=np.float32)
    for b in range(8):
        out[b] = res.results[b]["out"].T
    return out
